# version 07
# baseline (speedup 1.0000x reference)
import contextlib
import numpy as np
import ml_dtypes
import concourse.bass as bass
import concourse.mybir as mybir
from concourse.bass import broadcast_tensor_aps
from concourse.bass_core_rust import IndirectOffsetOnAxis
from concourse.bass_utils import run_bass_kernel_spmd

F32 = mybir.dt.float32
BF16 = mybir.dt.bfloat16
I32 = mybir.dt.int32
AF = mybir.ActivationFunctionType
OP = mybir.AluOpType
AX = mybir.AxisListType

NC = 8
D = 2048
T = 8192
TL = 1024
NM = 8
NH = 24
NPAIR = 12
DK = 64
KB = 8
HO = 1536
SEL = 1024
BH = 16
FFN = 8192
BIGI = 1 << 24
RG = [list(range(NC))]

ENGINES = ("pe", "act", "dve", "pool", "sp")


def later(*marks):
    best = {}
    def add(m):
        if m is None or (isinstance(m, (list, tuple)) and len(m) == 0):
            return
        if isinstance(m, tuple) and len(m) == 2 and isinstance(m[0], str):
            s, v = m
            if best.get(s, -1) < v:
                best[s] = v
        else:
            for x in m:
                add(x)
    for m in marks:
        add(m)
    return list(best.items())


class Prog:

    def __init__(self, nc, stack):
        self.nc = nc
        self._stack = stack
        self.sem = {}
        for name in ("tok_pe", "tok_act", "tok_dve", "tok_pool", "tok_sp",
                     "cc", "st"):
            self.sem[name] = stack.enter_context(nc.semaphore(name))
        self.cnt = {k: 0 for k in self.sem}
        self.steps = {e: [] for e in ENGINES}
        self.hw = {e: {} for e in ENGINES}
        self.tok_of = {"pe": "tok_pe", "act": "tok_act", "dve": "tok_dve",
                       "pool": "tok_pool", "sp": "tok_sp"}

    def _wl(self, eng, waits):
        out = []
        hw = self.hw[eng]
        for s, v in (later(waits) if isinstance(waits, list) else later(*waits)):
            if v <= 0 or hw.get(s, 0) >= v:
                continue
            hw[s] = v
            out.append((s, v))
        return out

    def op(self, eng, fn, waits=()):
        tok = self.tok_of[eng]
        w = later(waits if isinstance(waits, list) else list(waits))
        if eng in ("act", "dve", "pool") and self.cnt[tok] > 0:
            w = later(w, (tok, self.cnt[tok]))
        self.cnt[tok] += 1
        self.steps[eng].append((self._wl(eng, w), fn, (tok, 1)))
        return (tok, self.cnt[tok])

    def _qsem(self, stream):
        q = "dq_" + stream
        if q not in self.sem:
            self.sem[q] = self._stack.enter_context(self.nc.semaphore(q))
            self.cnt[q] = 0
        return q

    def dma(self, eng, stream, out, in_, waits=(), **kw):
        q = self._qsem(stream)
        self.cnt[q] += 16
        def fn(e, out=out, in_=in_, kw=kw):
            return e.dma_start(out=out, in_=in_, **kw)
        self.steps[eng].append((self._wl(eng, waits), fn, (q, 16)))
        return (q, self.cnt[q])

    def idma(self, stream, out, out_off, in_, in_off, waits=(), **kw):
        q = self._qsem(stream)
        if "bounds_check" in kw and isinstance(kw["bounds_check"], int):
            kw = dict(kw)
            kw["bounds_check"] = self.breg(kw["bounds_check"])
        self.cnt[q] += 16
        def fn(e, out=out, oo=out_off, in_=in_, io=in_off, kw=kw):
            return e.indirect_dma_start(out, oo, in_, io, **kw)
        self.steps["pool"].append((self._wl("pool", waits), fn, (q, 16)))
        return (q, self.cnt[q])

    def breg(self, value):
        if not hasattr(self, "_bregs"):
            self._bregs = {}
        if value not in self._bregs:
            reg = self.nc.alloc_register(mybir.EngineType.Pool, f"bnd{value}")
            def fn(e, reg=reg, value=value):
                return e.reg_mov(reg, value)
            self.op("pool", fn)
            self._bregs[value] = reg
        return self._bregs[value]

    def coll(self, kind, alu, ins, outs, waits=()):
        self.cnt["cc"] += 1
        def fn(e, kind=kind, alu=alu, ins=ins, outs=outs):
            return e.collective_compute(kind, alu, RG, ins, outs)
        self.steps["pool"].append((self._wl("pool", waits), fn, ("cc", 1)))
        return ("cc", self.cnt["cc"])

    def wait(self, eng, marks):
        self.steps[eng].append((self._wl(eng, marks), None, None))

    def fence(self):
        return [(k, v) for k, v in self.cnt.items() if v > 0]

    def emit(self, blk):
        engmap = {"pe": blk.tensor, "act": blk.scalar, "dve": blk.vector,
                  "pool": blk.gpsimd, "sp": blk.sync}
        for eng in ENGINES:
            steps = self.steps[eng]
            if not steps:
                continue
            sem = self.sem
            def body(e, steps=steps, sem=sem):
                for waits, fn, inc in steps:
                    for s, v in waits:
                        e.wait_ge(sem[s], v)
                    if fn is not None:
                        fn(e).then_inc(sem[inc[0]], inc[1])
            engmap[eng](body)


class Builder:
    def __init__(self, debug_stage=None, attn_limit=None):
        self.debug_stage = debug_stage
        self.attn_limit = attn_limit
        self.attn_parts = {"qk", "exp", "mask", "av", "acc"}
        self.nc = bass.Bass("TRN2", target_bir_lowering=False, debug=False,
                            num_devices=NC)
        self.dbg_names = []
        self.dbg_jobs = []

    def declare(self):
        nc = self.nc
        d = {}
        def di(name, shape, dt=F32):
            d[name] = nc.dram_tensor(name, list(shape), dt, kind="ExternalInput").ap()
        di("x_shard", (TL, D))
        di("x_full", (T, D))
        di("wq_m", (D, HO)); di("wk_m", (D, HO)); di("wv_m", (D, HO))
        di("wo_m", (HO, D))
        di("fk0t", (8, 14 * 4 * 128)); di("fv0t", (8, 14 * 4 * 128))
        di("fk1t", (8, 8 * 8 * 128)); di("fv1t", (8, 8 * 8 * 128))
        di("pk0", (8, 4 * 64)); di("pv0", (8, 4 * 64))
        di("pk1", (8, 8 * 64)); di("pv1", (8, 8 * 64))
        di("wrt", (128, 16))
        di("wq_b", (D, 256)); di("wk_b", (D, 256)); di("wv_b", (D, 256))
        di("wo_b", (D, 256))
        di("w1_b", (D, 1024)); di("w3_b", (D, 1024))
        di("w2_b", (FFN, 256))
        di("g1c", (128, 16)); di("g2c", (128, 2))
        di("cosf", (64, SEL)); di("sinf", (64, SEL))
        di("maskb", (SEL, SEL))
        di("diagm", (128, 1024), BF16)
        di("idnf", (128, 128))
        di("idnb", (128, 128), BF16)
        di("sltm", (128, 128))
        di("onesr", (1, 128))
        di("gidf", (128, 64))
        di("myrows", (128, 8), I32)
        di("mydims", (128, 2), I32)
        di("jiota", (128, 8), I32)
        self.out_y = nc.dram_tensor("y", [TL, D], F32, kind="ExternalOutput").ap()
        def dd(name, shape, dt=F32, space="Local"):
            d[name] = nc.dram_tensor(name, list(shape), dt,
                                     addr_space=space).ap()
        dd("weff_k", (NPAIR, D, 128), BF16)
        dd("weff_v", (NPAIR, D, 128), BF16)
        dd("kags", (KB * NPAIR * 128, 128), BF16)
        dd("kag", (NC * KB * NPAIR * 128, 128), BF16, space="Shared")
        dd("vags", (KB * 128, NH * 65), BF16)
        dd("vag", (NC * KB * 128, NH * 65), BF16, space="Shared")
        dd("lgag", (T, 1))
        dd("lgg", (T, 1))
        dd("seltab", (SEL, 2))
        dd("invd", (T, 1), I32)
        dd("attno_d", (TL, D))
        dd("xnself_d", (D, SEL))
        dd("avags", (256, SEL), BF16)
        dd("avag", (BH * 128, SEL), BF16, space="Shared")
        dd("ssqag", (1, SEL))
        dd("h2ags", (256, SEL), BF16)
        dd("h2ag", (D, SEL), BF16, space="Shared")
        dd("gags", (SEL, SEL), BF16)
        dd("gag", (FFN, SEL), BF16, space="Shared")
        dd("xwags", (SEL, 256))
        dd("xwag", (NC * SEL, 256), space="Shared")
        self.d = d

    def dbg(self, name, src_ap, marks):
        if self.debug_stage is None:
            return
        out = self.nc.dram_tensor("dbg_" + name, list(src_ap.shape), src_ap.dtype,
                                  kind="ExternalOutput").ap()
        self.dbg_names.append("dbg_" + name)
        self.dbg_jobs.append((out, src_ap,
                              later(marks) if marks is not None else None))

    def build(self):
        self.declare()
        nc = self.nc
        self._stack = contextlib.ExitStack()
        sb = self._stack.enter_context
        self.sb = sb
        self.p = Prog(nc, self._stack)
        self.c_idnf = sb(nc.sbuf_tensor("c_idnf", [128, 128], F32))
        self.c_idnb = sb(nc.sbuf_tensor("c_idnb", [128, 128], BF16))
        self.c_slt = sb(nc.sbuf_tensor("c_slt", [128, 128], F32))
        self.c_ones = sb(nc.sbuf_tensor("c_ones", [1, 128], F32))
        self.c_stats = sb(nc.sbuf_tensor("c_stats", [128, 16], F32))
        self.c_g1 = sb(nc.sbuf_tensor("c_g1", [128, 16], F32))
        self.c_wrt = sb(nc.sbuf_tensor("c_wrt", [128, 16], F32))
        self.lg_sb = sb(nc.sbuf_tensor("lg_sb", [1, TL], F32))
        self.selint = sb(nc.sbuf_tensor("selint", [128, 8], I32))
        self.wsel = sb(nc.sbuf_tensor("wsel", [128, 8], F32))
        self.myinv = sb(nc.sbuf_tensor("myinv", [128, 8 * NC], I32))

        stages = [("A", self.ph_a), ("ATTN", self.ph_attn), ("WO", self.ph_wo),
                  ("TOPK", self.ph_topk), ("BLK", self.ph_block),
                  ("EPI", self.ph_epilogue)]
        for si, (name, fn) in enumerate(stages):
            if si > 0:
                fence = self.p.fence()
                for e in ENGINES:
                    self.p.wait(e, fence)
            fn()
            if self.debug_stage == name or (
                    self.debug_stage == "AC" and name == "A"):
                break

        for st_name in ("_midstack2", "_midstack"):
            st = getattr(self, st_name, None)
            if st is not None:
                try:
                    st.close()
                except Exception:
                    pass
                setattr(self, st_name, None)
        if self.debug_stage == "EPI":
            for nm in ("attno_d", "xnself_d", "avag", "h2ag", "gag",
                       "xwag", "ssqag", "kag", "vag", "lgag"):
                self.dbg("D_" + nm, self.d[nm], None)
        fin = later(self.p.fence())
        for out, src, marks in self.dbg_jobs:
            self.p.dma("sp", "dbgS", out, src,
                       waits=marks if marks is not None else fin)
        self.p.wait("sp", self.p.fence())
        with nc.Block() as blk:
            self.p.emit(blk)
        self._stack.close()
        return nc

    def ph_a(self):
        nc, d, p = self.nc, self.d, self.p
        self._midstack = contextlib.ExitStack()
        self.q_sb = self._midstack.enter_context(
            nc.sbuf_tensor("q_sb", [128, NPAIR * TL], BF16))
        ph = contextlib.ExitStack()
        sb = ph.enter_context

        for dst, src_ in ((self.c_idnf, d["idnf"]), (self.c_idnb, d["idnb"]),
                          (self.c_slt, d["sltm"]), (self.c_ones, d["onesr"]),
                          (self.c_g1, d["g1c"]), (self.c_wrt, d["wrt"])):
            p.dma("sp", "cst", dst[:, :], src_)

        wfb = [sb(nc.sbuf_tensor(f"wfb{i}", [128, HO], F32)) for i in range(2)]
        web = [sb(nc.sbuf_tensor(f"web{i}", [128, HO], BF16)) for i in range(2)]
        fkb = [sb(nc.sbuf_tensor(f"fkb{i}", [8, 4 * 128 + 8 * 128], F32))
               for i in range(2)]
        pkb = sb(nc.sbuf_tensor("pkb", [8, 2 * (4 * 64 + 8 * 64)], F32))
        ptp = sb(nc.psum_tensor("ptp", [128, 1024], F32))
        plg = sb(nc.psum_tensor("plg", [1, 128], F32))
        foldscope = contextlib.ExitStack()
        pcor = foldscope.enter_context(nc.psum_tensor("pcor", [128, 768], F32))

        p.dma("sp", "cst", pkb[:, 0:256], d["pk0"])
        p.dma("sp", "cst", pkb[:, 256:768], d["pk1"])
        p.dma("sp", "cst", pkb[:, 768:1024], d["pv0"])
        m_pk = later(p.dma("sp", "cst", pkb[:, 1024:1536], d["pv1"]))
        self.m_const = m_pk

        fold_dve = {}
        fold_st = {}
        fold_fk = {}
        for which in range(2):
            wsrc = d["wk_m"] if which == 0 else d["wv_m"]
            wdst = d["weff_k"] if which == 0 else d["weff_v"]
            f0 = d["fk0t"] if which == 0 else d["fv0t"]
            f1 = d["fk1t"] if which == 0 else d["fv1t"]
            pko = 768 * which
            for i in range(16):
                ci = which * 16 + i
                slot = ci % 2
                war = [fold_dve[ci - 2]] if ci >= 2 else []
                m_ld = p.dma("sp", f"wf{slot}", wfb[slot][:, :],
                             wsrc[128 * i:128 * (i + 1), :], waits=later(*war))
                mfk = []
                if i >= 2:
                    mfk.append(p.dma("sp", f"fk{slot}", fkb[slot][:, 0:512],
                                     f0[:, 512 * (i - 2):512 * (i - 1)],
                                     waits=later(*([fold_fk[ci - 2]] if ci >= 2 else []))))
                if i >= 8:
                    mfk.append(p.dma("sp", f"fk{slot}", fkb[slot][:, 512:1536],
                                     f1[:, 1024 * (i - 8):1024 * (i - 7)]))
                m_mm = None
                pw = later(m_ld, m_pk, *mfk,
                           *([fold_dve[ci - 1]] if ci >= 1 else []))
                if i >= 2:
                    for h in range(4):
                        m_mm = p.op("pe", lambda t, h=h, slot=slot, pko=pko: t.matmul(
                            pcor[:, 64 * h:64 * h + 64],
                            fkb[slot][:, 128 * h:128 * (h + 1)],
                            pkb[:, pko + 64 * h:pko + 64 * h + 64],
                            start=True, stop=True), waits=pw)
                        pw = ()
                if i >= 8:
                    for h in range(8):
                        m_mm = p.op("pe", lambda t, h=h, slot=slot, pko=pko: t.matmul(
                            pcor[:, 256 + 64 * h:256 + 64 * h + 64],
                            fkb[slot][:, 512 + 128 * h:512 + 128 * (h + 1)],
                            pkb[:, pko + 256 + 64 * h:pko + 256 + 64 * h + 64],
                            start=True, stop=True), waits=pw)
                        pw = ()
                fold_fk[ci] = m_mm
                vw = later(m_ld, m_mm, *([fold_st[ci - 2]] if ci >= 2 else []))
                if i >= 2:
                    m1 = p.op("dve", lambda v, slot=slot: v.scalar_tensor_tensor(
                        web[slot][:, 0:256], wfb[slot][:, 0:256], 1.0,
                        pcor[:, 0:256], OP.mult, OP.add), waits=vw)
                else:
                    m1 = p.op("dve", lambda v, slot=slot: v.tensor_copy(
                        web[slot][:, 0:256], wfb[slot][:, 0:256]), waits=vw)
                if i >= 8:
                    p.op("dve", lambda v, slot=slot: v.scalar_tensor_tensor(
                        web[slot][:, 256:768], wfb[slot][:, 256:768], 1.0,
                        pcor[:, 256:768], OP.mult, OP.add), waits=later(m1))
                else:
                    p.op("dve", lambda v, slot=slot: v.tensor_copy(
                        web[slot][:, 256:768], wfb[slot][:, 256:768]),
                        waits=later(m1))
                m3 = p.op("dve", lambda v, slot=slot: v.tensor_copy(
                    web[slot][:, 768:1536], wfb[slot][:, 768:1536]))
                fold_dve[ci] = m3
                stm = None
                for oc in range(NPAIR):
                    stm = p.dma("pool", f"weS{slot}", wdst[oc, 128 * i:128 * (i + 1), :],
                                web[slot][:, 128 * oc:128 * (oc + 1)],
                                waits=later(m3) if oc == 0 else ())
                fold_st[ci] = stm
        self.m_fold_done = later(fold_st[30], fold_st[31])
        foldscope.close()

        xb = [sb(nc.sbuf_tensor(f"xb{i}", [128, D], F32)) for i in range(2)]
        xnb = [sb(nc.sbuf_tensor(f"xnb{i}", [128, D], F32)) for i in range(2)]
        xc32 = [sb(nc.sbuf_tensor(f"xc32{i}", [128, 128], F32)) for i in range(2)]
        xn_bf = sb(nc.sbuf_tensor("xn_bf", [128, 16 * TL], BF16))
        ssq = sb(nc.sbuf_tensor("ssq", [128, 1], F32))
        sq_scr = sb(nc.sbuf_tensor("sq_scr", [128, D], F32))
        tst = sb(nc.sbuf_tensor("tst", [128, 4], F32))

        tile_done = {}
        ev = {}
        lgmm = {}
        m_lg_prev = None
        ssq_rd = None
        for m in range(NM):
            slot = m % 2
            war = [tile_done[m - 2]] if m >= 2 else []
            m_ld = p.dma("sp", f"x{slot}", xb[slot][:, :],
                         d["x_shard"][128 * m:128 * (m + 1), :], waits=later(*war))
            m_sq = p.op("act", lambda s, slot=slot: s.activation(
                sq_scr[:, :], xb[slot][:, :], AF.Square, accum_out=ssq[:, :]),
                waits=later(m_ld, ssq_rd))
            m_sum = p.op("dve", lambda v, slot=slot: v.reduce_sum(
                tst[:, 0:1], xb[slot][:, :], AX.X), waits=later(m_ld))
            m_mean = p.op("dve", lambda v, m=m: v.tensor_scalar(
                self.c_stats[:, 2 * m:2 * m + 1], tst[:, 0:1], 1.0 / D, None,
                OP.mult), waits=later(m_sum, self.m_const))
            m_v1 = p.op("dve", lambda v: v.tensor_scalar(
                tst[:, 1:2], ssq[:, :], 1.0 / D, None, OP.mult), waits=later(m_sq))
            ssq_rd = m_v1
            m_v2 = p.op("dve", lambda v, m=m: v.tensor_mul(
                tst[:, 2:3], self.c_stats[:, 2 * m:2 * m + 1],
                self.c_stats[:, 2 * m:2 * m + 1]), waits=later(m_mean))
            m_v3 = p.op("dve", lambda v: v.tensor_sub(
                tst[:, 1:2], tst[:, 1:2], tst[:, 2:3]), waits=later(m_v1, m_v2))
            m_v4 = p.op("dve", lambda v: v.tensor_scalar(
                tst[:, 1:2], tst[:, 1:2], 1e-5, None, OP.add), waits=later(m_v3))
            m_rc = p.op("dve", lambda v: v.reciprocal(tst[:, 2:3], tst[:, 1:2]),
                        waits=later(m_v4))
            m_rstd = p.op("act", lambda s, m=m: s.activation(
                self.c_stats[:, 2 * m + 1:2 * m + 2], tst[:, 2:3], AF.Sqrt),
                waits=later(m_rc))
            m_xn = p.op("dve", lambda v, slot=slot, m=m: v.tensor_scalar(
                xnb[slot][:, :], xb[slot][:, :],
                self.c_stats[:, 2 * m:2 * m + 1],
                self.c_stats[:, 2 * m + 1:2 * m + 2], OP.subtract, OP.mult),
                waits=later(m_rstd))
            for i in range(16):
                ps = i % 2
                war_tp = [ev[(m, i - 2)]] if i >= 2 else (
                    [ev[(m - 1, 14 + i)]] if m >= 1 else [])
                m_tp = p.op("pe", lambda t, slot=slot, i=i, ps=ps: t.transpose(
                    ptp[:, 512 * ps:512 * ps + 128],
                    xnb[slot][:, 128 * i:128 * (i + 1)], self.c_idnf[:, :]),
                    waits=later(m_xn, self.m_const, *war_tp))
                war_x = [lgmm[(m, i - 2)]] if i >= 2 else (
                    [lgmm[(m - 1, 14 + i)]] if m >= 1 else [])
                m_e1 = p.op("dve", lambda v, ps=ps: v.tensor_copy(
                    xc32[ps][:, :], ptp[:, 512 * ps:512 * ps + 128]),
                    waits=later(m_tp, *war_x))
                m_e2 = p.op("dve", lambda v, ps=ps, m=m, i=i: v.tensor_copy(
                    xn_bf[:, TL * i + 128 * m:TL * i + 128 * (m + 1)],
                    xc32[ps][:, :]), waits=later(m_e1))
                ev[(m, i)] = m_e2
                lw = later(m_e1, *([m_lg_prev] if (i == 0 and m_lg_prev) else []))
                lgmm[(m, i)] = p.op("pe", lambda t, ps=ps, i=i: t.matmul(
                    plg[0:1, :], self.c_wrt[:, i:i + 1], xc32[ps][:, :],
                    start=(i == 0), stop=(i == 15)), waits=lw)
            m_lg = p.op("dve", lambda v, m=m: v.tensor_copy(
                self.lg_sb[0:1, 128 * m:128 * (m + 1)], plg[0:1, :]),
                waits=later(lgmm[(m, 15)]))
            m_lg_prev = m_lg
            tile_done[m] = later(ev[(m, 15)], m_lg, m_sq, m_sum)
        self.m_xnbf = ev[(7, 15)]
        m_lgst = p.dma("sp", "lgS",
                       d["lgag"].rearrange("(a t) o -> a (t o)", a=1)[0:1, 0:TL],
                       self.lg_sb[0:1, :], waits=later(m_lg_prev))

        wcb = [sb(nc.sbuf_tensor(f"wcb{i}", [128, 16 * 128], BF16)) for i in range(2)]
        kst = sb(nc.sbuf_tensor("kst", [128, TL], BF16))
        vst = sb(nc.sbuf_tensor("vst", [128, TL], BF16))
        vtm = sb(nc.sbuf_tensor("vtm", [128, KB * 1560], BF16))
        pprj = sb(nc.psum_tensor("pprj", [128, TL], F32))
        ptv = sb(nc.psum_tensor("ptv", [128, 2048], BF16))

        m_vtm1 = p.op("dve", lambda v: v.memset(vtm[:, :], 1.0))

        kst_free = None
        vst_free = None
        vtm_ev = {}
        prev_ev = None
        wld = {}
        for which in range(3):
            for oc in range(NPAIR):
                key = which * NPAIR + oc
                slot = key % 2
                war = [wld[key - 2]] if key >= 2 else []
                if which < 2:
                    wsrc = d["weff_k"] if which == 0 else d["weff_v"]
                    extra = self.m_fold_done if key == 0 else []
                    m_w = p.dma("pool", f"wc{slot}",
                                wcb[slot][:, :].rearrange("p (dc o) -> p dc o", o=128),
                                wsrc[oc].rearrange("(dc p) o -> p dc o", p=128),
                                waits=later(*war, extra))
                    dcmax = 16
                else:
                    dcmax = 2 if oc < 2 else (8 if oc < 6 else 16)
                    m_w = p.dma("pool", f"wc{slot}",
                                wcb[slot][:, 0:dcmax * 128]
                                .rearrange("p (dc o) -> p dc o", o=128),
                                d["wq_m"][0:dcmax * 128, 128 * oc:128 * (oc + 1)]
                                .rearrange("(dc p) o -> p dc o", p=128),
                                waits=later(*war))
                mm = None
                w0 = later(m_w, self.m_xnbf, *([prev_ev] if prev_ev else []))
                for dc in range(dcmax):
                    for half in range(2):
                        mm = p.op("pe", lambda t, slot=slot, dc=dc, half=half,
                                  dcmax=dcmax: t.matmul(
                            pprj[:, 512 * half:512 * (half + 1)],
                            wcb[slot][:, 128 * dc:128 * (dc + 1)],
                            xn_bf[:, TL * dc + 512 * half:TL * dc + 512 * (half + 1)],
                            start=(dc == 0), stop=(dc == dcmax - 1)), waits=w0)
                        w0 = ()
                wld[key] = mm
                if which == 0:
                    wk = [kst_free] if kst_free else []
                    m_ev = p.op("dve", lambda v: v.tensor_copy(kst[:, :], pprj[:, :]),
                                waits=later(mm, *wk))
                    stm = None
                    for mp in range(NM):
                        stm = p.dma("sp", "kagS",
                                    d["kags"][(mp * NPAIR + oc) * 128:
                                              (mp * NPAIR + oc) * 128 + 128, :],
                                    kst[:, 128 * mp:128 * (mp + 1)],
                                    waits=later(m_ev) if mp == 0 else ())
                    kst_free = stm
                    prev_ev = m_ev
                elif which == 1:
                    wk = [vst_free] if vst_free else []
                    m_ev = p.op("dve", lambda v: v.tensor_copy(vst[:, :], pprj[:, :]),
                                waits=later(mm, *wk))
                    last = None
                    for mp in range(NM):
                        ps = mp % 2
                        war_t = [vtm_ev[(oc, mp - 2)]] if mp >= 2 else (
                            [vtm_ev[(oc - 1, 6 + mp)]] if oc >= 1 else [])
                        m_t = p.op("pe", lambda t, mp=mp, ps=ps: t.transpose(
                            ptv[:, 1024 * ps:1024 * ps + 128],
                            vst[:, 128 * mp:128 * (mp + 1)], self.c_idnb[:, :]),
                            waits=later(m_ev, *war_t))
                        m_c1 = p.op("dve", lambda v, mp=mp, ps=ps, oc=oc: v.tensor_copy(
                            vtm[:, 1560 * mp + 65 * 2 * oc:1560 * mp + 65 * 2 * oc + 64],
                            ptv[:, 1024 * ps:1024 * ps + 64]),
                            waits=later(m_t, m_vtm1))
                        m_c2 = p.op("dve", lambda v, mp=mp, ps=ps, oc=oc: v.tensor_copy(
                            vtm[:, 1560 * mp + 65 * (2 * oc + 1):
                                1560 * mp + 65 * (2 * oc + 1) + 64],
                            ptv[:, 1024 * ps + 64:1024 * ps + 128]),
                            waits=later(m_c1))
                        vtm_ev[(oc, mp)] = m_c2
                        last = m_c2
                    vst_free = last
                    prev_ev = last
                else:
                    m_ev = p.op("dve", lambda v, oc=oc: v.tensor_copy(
                        self.q_sb[:, TL * oc:TL * (oc + 1)], pprj[:, :]),
                        waits=later(mm))
                    prev_ev = m_ev
        self.m_q_done = prev_ev
        m_vtm_done = vtm_ev[(NPAIR - 1, NM - 1)]
        vag_last = None
        for mp in range(NM):
            vag_last = p.dma("sp", "vagS", d["vags"][128 * mp:128 * (mp + 1), :],
                             vtm[:, 1560 * mp:1560 * (mp + 1)],
                             waits=later(m_vtm_done) if mp == 0 else ())

        agw = later(kst_free, vag_last, m_lgst)
        if self.debug_stage in ("A",):
            self.m_ag_kv = agw
            self.m_ag_lg = agw
        else:
            a1 = p.coll("AllGather", OP.bypass,
                        [d["kags"]], [d["kag"]], waits=agw)
            a2 = p.coll("AllGather", OP.bypass,
                        [d["vags"]], [d["vag"]])
            a3 = p.coll("AllGather", OP.bypass,
                        [d["lgag"][0:TL, :]], [d["lgag"]])
            self.m_ag_kv = later(a1, a2)
            self.m_ag_lg = later(a3)

        self.dbg("stats", self.c_stats[:, :], tile_done[7])
        self.dbg("lg", self.lg_sb[:, :], [m_lg_prev])
        self.dbg("xnbf", xn_bf[:, 0:2 * TL], [self.m_xnbf])
        self.dbg("qsb", self.q_sb[:, 0:TL], [self.m_q_done])
        self.dbg("vtm", vtm[:, 0:1560], [m_vtm_done])

        self.m_ph_a = p.fence()
        ph.close()

    def ph_attn(self):
        nc, d, p = self.nc, self.d, self.p
        self._midstack2 = contextlib.ExitStack()
        self.acc65 = [self._midstack2.enter_context(
            nc.sbuf_tensor(f"acc65_{pr}", [65, 2048], F32)) for pr in range(NPAIR)]
        ph = contextlib.ExitStack()
        sb = ph.enter_context

        ktile = [sb(nc.sbuf_tensor(f"ktile{i}", [128, 6 * TL], BF16)) for i in range(2)]
        vtile = [sb(nc.sbuf_tensor(f"vtile{i}", [128, 8 * 780], BF16)) for i in range(2)]
        pt = [sb(nc.sbuf_tensor(f"pt{i}", [128, 1536], BF16)) for i in range(3)]
        diag_sb = sb(nc.sbuf_tensor("diag_sb", [128, 1024], BF16))
        psc = [sb(nc.psum_tensor(f"psc{i}", [128, 1536], F32)) for i in range(2)]
        pav = sb(nc.psum_tensor("pav", [65, 1024], F32))

        m_diag = p.dma("sp", "cst2", diag_sb[:, :], d["diagm"])
        m_iz = [p.op("dve", lambda v, r=r: v.memset(psc[r][:, :], 0.0))
                for r in range(2)]

        def halves_of(kb):
            qwf = 1024 - 128 * kb
            hs = [(128 * kb, 128, True)]
            off = 128 * kb + 128
            rem = qwf - 128
            for w in (512, 256, 128):
                while rem >= w:
                    hs.append((off, w, False))
                    off += w
                    rem -= w
            return hs

        tile_ld = {}
        tile_used = {}
        exp_done_h = {}
        pt_used = {}
        region_idx = 0
        pav_free = None
        acc_last = {}

        LP, LK, LPR = self.attn_limit or (2, KB, 6)
        for pass_ in range(LP):
            for kb in range(LK):
                ti = pass_ * KB + kb
                slot = ti % 2
                ldw = [tile_used[ti - 2]] if ti >= 2 else []
                if ti == 0:
                    ldw = [self.m_ag_kv]
                lds = []
                kt3 = ktile[slot][:, :].rearrange("p (oc t) -> p oc t", t=TL)
                for j in range(KB):
                    rowb = j * (KB * NPAIR * 128) + (kb * NPAIR + pass_ * 6) * 128
                    lds.append(p.dma("sp", f"kt{slot}",
                                     kt3[:, :, 128 * j:128 * (j + 1)],
                                     d["kag"][rowb:rowb + 6 * 128, :]
                                     .rearrange("(oc p) t -> p oc t", p=128),
                                     waits=later(*ldw) if j == 0 else ()))
                for j in range(KB):
                    lds.append(p.dma("sp", f"vt{slot}",
                                     vtile[slot][:, 780 * j:780 * (j + 1)],
                                     d["vag"][j * KB * 128 + kb * 128:
                                              j * KB * 128 + (kb + 1) * 128,
                                              780 * pass_:780 * (pass_ + 1)],
                                     waits=later(*ldw) if j == 0 else ()))
                m_ld = later(*lds)
                tile_ld[ti] = m_ld

                for pl in range(LPR):
                    pair = pass_ * 6 + pl
                    hs = halves_of(kb)
                    groups = {}
                    for (qoff, w, dg) in hs:
                        lst = groups.setdefault(w, [])
                        for h in range(2):
                            for j in range(KB):
                                lst.append((j, h, qoff, dg))
                    last_mm = None
                    av_cnt = {}
                    av_pending = []

                    def flush_av(job, slot=slot, pair=pair, kb=kb):
                        nonlocal last_mm, pav_free
                        reg, w, r3, me, mask_marks, ridx, pl_ = (
                            job["reg"], job["w"], job["r3"], job["me"],
                            job["mask"], job["ridx"], job["pl"])
                        avw = later(me, *mask_marks)
                        for si, (j, h, qoff, dg) in enumerate(reg):
                            if "av" not in self.attn_parts:
                                last_mm = later(me)
                                break
                            keyh = (h, qoff)
                            cnt = av_cnt.get(keyh, 0)
                            w_mm = later(avw, *( [pav_free] if cnt == 0 else [] ))
                            mv = p.op("pe", lambda t, slot=slot, j=j, h=h, pl=pl_,
                                      si=si, w=w, r3=r3, cnt=cnt:
                                t.matmul(
                                    pav[:, 512 * h:512 * h + w],
                                    vtile[slot][:, 780 * j + 65 * (2 * pl + h):
                                                780 * j + 65 * (2 * pl + h) + 65],
                                    pt[r3][:, si * w:si * w + w],
                                    start=(cnt == 0), stop=(cnt == 7)), waits=w_mm)
                            avw = ()
                            av_cnt[keyh] = cnt + 1
                            last_mm = mv
                            if cnt == 7:
                                aw = later(mv)
                                dst0 = 1024 * h + qoff
                                if kb == 0:
                                    ma = p.op("dve", lambda v, pair=pair, h=h,
                                              qoff=qoff, w=w, dst0=dst0: v.tensor_copy(
                                        self.acc65[pair][:, dst0:dst0 + w],
                                        pav[:, 512 * h:512 * h + w]), waits=aw)
                                else:
                                    ma = p.op("dve", lambda v, pair=pair, h=h,
                                              qoff=qoff, w=w, dst0=dst0: v.scalar_tensor_tensor(
                                        self.acc65[pair][:, dst0:dst0 + w],
                                        pav[:, 512 * h:512 * h + w], 1.0,
                                        self.acc65[pair][:, dst0:dst0 + w],
                                        OP.mult, OP.add), waits=aw)
                                pav_free = later(pav_free, ma)
                                acc_last[pair] = ma
                        pt_used[ridx] = later(*mask_marks, last_mm)
                    for w in (128, 256, 512):
                        if w not in groups:
                            continue
                        cap = 1536 // w
                        slots = groups[w]
                        regions = []
                        run = []
                        for s_ in slots:
                            if run and s_[1] != run[-1][1]:
                                regions.extend(run[i:i + cap]
                                               for i in range(0, len(run), cap))
                                run = []
                            run.append(s_)
                        if run:
                            regions.extend(run[i:i + cap]
                                           for i in range(0, len(run), cap))
                        for reg in regions:
                            rlim = getattr(self, "attn_rlim", None)
                            if rlim is not None and region_idx >= rlim:
                                region_idx += 1
                                continue
                            r2 = reg[0][1]
                            r3 = region_idx % 3
                            qk_w = later(
                                m_ld, m_iz,
                                exp_done_h.get(r2))
                            mq = None
                            for si, (j, h, qoff, dg) in enumerate(reg):
                                if not any(x in self.attn_parts for x in ("qk", "qk1")):
                                    break
                                if "qk1" in self.attn_parts and si > 0:
                                    break
                                mq = p.op("pe", lambda t, slot=slot, pl=pl, j=j, h=h,
                                          pair=pair, qoff=qoff, w=w, si=si, r2=r2:
                                    t.matmul(
                                        psc[r2][:, si * w:(si + 1) * w],
                                        ktile[slot][64 * h:64 * h + 64,
                                                    pl * TL + 128 * j:pl * TL + 128 * (j + 1)],
                                        self.q_sb[64 * h:64 * h + 64,
                                                  pair * TL + qoff:pair * TL + qoff + w],
                                        start=True, stop=True), waits=qk_w)
                                qk_w = ()
                            ptw = pt_used.get(region_idx - 3, [])
                            fd = len(reg) * w
                            if "exp" in self.attn_parts:
                                me = p.op("act", lambda s, r2=r2, r3=r3, fd=fd: s.activation(
                                    pt[r3][:, 0:fd], psc[r2][:, 0:fd], AF.Exp, scale=0.125),
                                    waits=later(mq, ptw))
                            else:
                                me = p.op("dve", lambda v, r3=r3, fd=fd: v.memset(
                                    pt[r3][:, 0:fd], 0.125), waits=later(mq, ptw))
                            exp_done_h[r2] = me
                            mask_marks = []
                            run = None
                            runs = []
                            lastj = None
                            for si, (j, h, qoff, dg) in enumerate(reg):
                                if dg and run is not None and j == lastj + 1:
                                    run[1] = si
                                    lastj = j
                                elif dg:
                                    if run is not None:
                                        runs.append(run)
                                    run = [si, si, j]
                                    lastj = j
                                else:
                                    if run is not None:
                                        runs.append(run); run = None
                            if run is not None:
                                runs.append(run)
                            for (s0, s1, j0) in (runs if "mask" in self.attn_parts else []):
                                n = s1 - s0 + 1
                                mask_marks.append(p.op("dve",
                                    lambda v, s0=s0, n=n, j0=j0, r3=r3, w=w: v.tensor_mul(
                                        pt[r3][:, s0 * w:(s0 + n) * w],
                                        pt[r3][:, s0 * w:(s0 + n) * w],
                                        diag_sb[:, 128 * j0:128 * (j0 + n)]),
                                    waits=later(me, m_diag)))
                            av_pending.append(dict(
                                reg=reg, w=w, r3=r3, me=me, mask=mask_marks,
                                ridx=region_idx, pl=pl))
                            if len(av_pending) > 1:
                                flush_av(av_pending.pop(0))
                            region_idx += 1
                    for job in av_pending:
                        flush_av(job)
                    av_pending = []
                    tile_used[ti] = last_mm
        self.m_acc = later(*[acc_last[pr] for pr in acc_last])
        self.dbg("acc0", self.acc65[0][:, :], self.m_acc)
        self.dbg("acc7", self.acc65[7][:, :], self.m_acc)
        self.m_ph_attn = p.fence()
        ph.close()

    def ph_wo(self):
        nc, d, p = self.nc, self.d, self.p
        ph = contextlib.ExitStack()
        sb = ph.enter_context

        cat = sb(nc.sbuf_tensor("cat", [128, NPAIR * TL], BF16))
        dnr = sb(nc.sbuf_tensor("dnr", [1, 2048], F32))
        wob = [sb(nc.sbuf_tensor(f"wob{i}", [128, 12 * 128], BF16)) for i in range(2)]
        wo32 = sb(nc.sbuf_tensor("wo32", [128, TL], F32))
        att_st = [sb(nc.sbuf_tensor(f"attst{i}", [128, 128], F32)) for i in range(2)]
        pbc = sb(nc.psum_tensor("pbc", [64, 2048], F32))
        pwo = sb(nc.psum_tensor("pwo", [128, TL], F32))
        ptw = sb(nc.psum_tensor("ptw", [128, 1024], F32))

        cat_done = {}
        m_rc_prev = None
        m_bc_prev = {}
        for pr in range(NPAIR):
            m_rc = p.op("dve", lambda v, pr=pr: v.reciprocal(
                dnr[0:1, :], self.acc65[pr][64:65, :]),
                waits=later(self.m_acc, *( [m_bc_prev.get(pr - 1)] if pr else [] )))
            bc = None
            for q4 in range(4):
                bc = p.op("pe", lambda t, q4=q4: t.matmul(
                    pbc[:, 512 * q4:512 * (q4 + 1)], self.c_ones[0:1, 0:64],
                    dnr[0:1, 512 * q4:512 * (q4 + 1)], start=True, stop=True),
                    waits=later(m_rc, *( [cat_done[pr - 1]] if pr else [] )) if q4 == 0 else ())
            m_bc_prev[pr] = bc
            m1 = p.op("dve", lambda v, pr=pr: v.tensor_mul(
                cat[0:64, pr * TL:pr * TL + TL],
                self.acc65[pr][0:64, 0:1024], pbc[0:64, 0:1024]),
                waits=later(bc))
            m2 = p.op("dve", lambda v, pr=pr: v.tensor_mul(
                cat[64:128, pr * TL:pr * TL + TL],
                self.acc65[pr][0:64, 1024:2048], pbc[0:64, 1024:2048]),
                waits=later(m1))
            cat_done[pr] = m2
        m_cat = cat_done[NPAIR - 1]

        wo_ld = {}
        attno_st = []
        prev_ev = None
        for oc in range(16):
            slot = oc % 2
            dcmax = 3 if oc < 2 else (6 if oc < 8 else 12)
            war = [wo_ld[oc - 2]] if oc >= 2 else []
            m_w = p.dma("pool", f"wo{slot}",
                        wob[slot][:, 0:dcmax * 128]
                        .rearrange("p (dc o) -> p dc o", o=128),
                        d["wo_m"][0:dcmax * 128, 128 * oc:128 * (oc + 1)]
                        .rearrange("(dc p) o -> p dc o", p=128),
                        waits=later(*war))
            mm = None
            w0 = later(m_w, m_cat, *( [prev_ev] if prev_ev else [] ))
            for dc in range(dcmax):
                for half in range(2):
                    mm = p.op("pe", lambda t, slot=slot, dc=dc, half=half,
                              dcmax=dcmax: t.matmul(
                        pwo[:, 512 * half:512 * (half + 1)],
                        wob[slot][:, 128 * dc:128 * (dc + 1)],
                        cat[:, TL * dc + 512 * half:TL * dc + 512 * (half + 1)],
                        start=(dc == 0), stop=(dc == dcmax - 1)), waits=w0)
                    w0 = ()
            wo_ld[oc] = mm
            lastT = attno_st[-1][0] if attno_st else None
            m_ev = p.op("dve", lambda v: v.tensor_copy(wo32[:, :], pwo[:, :]),
                        waits=later(mm, prev_ev, lastT))
            prev_ev = m_ev
            for mp in range(NM):
                ps = mp % 2
                tw = [attno_st[-2][1]] if len(attno_st) >= 2 else []
                m_t = p.op("pe", lambda t, mp=mp, ps=ps: t.transpose(
                    ptw[:, 512 * ps:512 * ps + 128],
                    wo32[:, 128 * mp:128 * (mp + 1)], self.c_idnf[:, :]),
                    waits=later(m_ev, *tw))
                m_c = p.op("dve", lambda v, ps=ps: v.tensor_copy(
                    att_st[ps][:, :], ptw[:, 512 * ps:512 * ps + 128]),
                    waits=later(m_t, *( [attno_st[-2][2]] if len(attno_st) >= 2 else [] )))
                m_s = p.dma("sp", f"ao{ps}",
                            d["attno_d"][128 * mp:128 * (mp + 1),
                                         128 * oc:128 * (oc + 1)],
                            att_st[ps][:, :], waits=later(m_c))
                attno_st.append((m_t, m_c, m_s))
        self.m_attno = later(attno_st[-1][2], attno_st[-2][2])
        self.m_ph_wo = p.fence()
        ph.close()
        self._midstack2.close()
        self._midstack2 = None
        self._midstack.close()
        self._midstack = None

    def ph_topk(self):
        nc, d, p = self.nc, self.d, self.p
        ph = contextlib.ExitStack()
        sb = ph.enter_context

        lgt = sb(nc.sbuf_tensor("lgt", [128, 64], F32))
        key = sb(nc.sbuf_tensor("key", [128, 64], I32))
        tmpi = sb(nc.sbuf_tensor("tmpi", [128, 64], I32))
        ge = sb(nc.sbuf_tensor("ge", [128, 64], I32))
        cntf = sb(nc.sbuf_tensor("cntf", [128, 1], F32))
        gef = sb(nc.sbuf_tensor("gef", [128, 64], F32))
        tot = sb(nc.sbuf_tensor("tot", [1, 1], F32))
        totb = sb(nc.sbuf_tensor("totb", [128, 1], F32))
        seli = sb(nc.sbuf_tensor("seli", [128, 1], I32))
        theta = sb(nc.sbuf_tensor("theta", [128, 1], I32))
        thtest = sb(nc.sbuf_tensor("thtest", [128, 1], I32))
        db = sb(nc.sbuf_tensor("db", [128, 1], F32))
        m1f = sb(nc.sbuf_tensor("m1f", [128, 64], F32))
        m2f = sb(nc.sbuf_tensor("m2f", [128, 64], F32))
        fm = sb(nc.sbuf_tensor("fm", [128, 64], F32))
        sc1 = sb(nc.sbuf_tensor("sc1", [128, 64], F32))
        sc2 = sb(nc.sbuf_tensor("sc2", [128, 64], F32))
        offc = sb(nc.sbuf_tensor("offc", [128, 1], F32))
        expv = sb(nc.sbuf_tensor("expv", [128, 64], F32))
        wv = sb(nc.sbuf_tensor("wv", [128, 64], F32))
        pairs = sb(nc.sbuf_tensor("pairs", [128, 128], F32))
        tposf = sb(nc.sbuf_tensor("tposf", [128, 64], F32))
        tposi = sb(nc.sbuf_tensor("tposi", [128, 64], I32))
        gidf_sb = sb(nc.sbuf_tensor("gidf_sb", [128, 64], F32))
        onesc = sb(nc.sbuf_tensor("onesc", [128, 1], F32))
        selw = sb(nc.sbuf_tensor("selw", [128, 16], F32))
        itile = sb(nc.sbuf_tensor("itile", [128, 64], I32))
        jio = sb(nc.sbuf_tensor("jio", [128, 8], I32))
        myr = sb(nc.sbuf_tensor("myr", [128, 8], I32))
        pts = sb(nc.psum_tensor("pts", [1, 64], F32))
        ptb = sb(nc.psum_tensor("ptb", [128, 64], F32))

        lgg3 = d["lgg"].rearrange("(m c q) o -> c m (q o)", m=8, c=8)
        m_pg = None
        for cp in range(8):
            m_pg = p.dma("sp", "tkP", lgg3[cp:cp + 1, :, :],
                         d["lgag"][1024 * cp:1024 * (cp + 1), :]
                         .rearrange("(a m q) o -> a m (q o)", a=1, m=8),
                         waits=later(self.m_ag_lg) if cp == 0 else ())
        m_lg = p.dma("sp", "tkA", lgt[:, :],
                     d["lgg"].rearrange("(p f) o -> p (f o)", p=128),
                     waits=later(m_pg))
        p.dma("sp", "tkC", gidf_sb[:, :], d["gidf"])
        p.dma("sp", "tkC", jio[:, :], d["jiota"])
        m_tkc = p.dma("sp", "tkC", myr[:, :], d["myrows"])
        m_gid = m_ji = m_my = m_tkc
        m_oc = p.op("dve", lambda v: v.memset(onesc[:, :], 1.0))

        lgi = lgt[:, :].bitcast(I32)
        k1 = p.op("dve", lambda v: v.tensor_scalar(
            tmpi[:, :], lgi, 31, None, OP.arith_shift_right), waits=later(m_lg))
        k2 = p.op("dve", lambda v: v.tensor_scalar(
            tmpi[:, :], tmpi[:, :], 0x7FFFFFFF, None, OP.bitwise_and),
            waits=later(k1))
        k3 = p.op("dve", lambda v: v.tensor_tensor(
            key[:, :], lgi, tmpi[:, :], OP.bitwise_xor), waits=later(k2))

        def count_ge(th_ap, opx, wait0):
            def cmp(v, th_ap=th_ap, opx=opx):
                a, b = broadcast_tensor_aps(key[:, :], th_ap)
                return v.tensor_tensor(ge[:, :], a, b, opx)
            c1 = p.op("dve", cmp, waits=wait0)
            c2 = p.op("dve", lambda v: v.tensor_copy(gef[:, :], ge[:, :]),
                      waits=later(c1))
            c3 = p.op("dve", lambda v: v.reduce_sum(
                cntf[:, :], gef[:, :], AX.X), waits=later(c2))
            c4 = p.op("pe", lambda t: t.matmul(
                pts[0:1, 0:1], onesc[:, :], cntf[:, :], start=True, stop=True),
                waits=later(c3, m_oc))
            c5 = p.op("dve", lambda v: v.tensor_copy(tot[:, :], pts[0:1, 0:1]),
                      waits=later(c4))
            c6 = p.op("pe", lambda t: t.matmul(
                ptb[:, 0:1], self.c_ones[0:1, :], tot[:, :], start=True, stop=True),
                waits=later(c5))
            c7 = p.op("dve", lambda v: v.tensor_copy(totb[:, :], ptb[:, 0:1]),
                      waits=later(c6))
            return c7

        m_th = None
        prev = later(k3)
        for b in range(31, -1, -1):
            if b == 31:
                t1 = p.op("dve", lambda v: v.memset(thtest[:, :], 0), waits=prev)
                m_th = p.op("dve", lambda v: v.memset(theta[:, :], -2147483648),
                            waits=prev)
            else:
                t1 = p.op("dve", lambda v, b=b: v.tensor_scalar(
                    thtest[:, :], theta[:, :], 1 << b, None, OP.add),
                    waits=later(m_th, prev))
            c7 = count_ge(thtest[:, 0:1], OP.is_ge, later(t1))
            s1 = p.op("dve", lambda v: v.tensor_scalar(
                db[:, :], totb[:, :], 1024.0, None, OP.is_ge), waits=later(c7))
            s2 = p.op("dve", lambda v: v.tensor_copy(seli[:, :], db[:, :]),
                      waits=later(s1))
            if b == 31:
                u1 = p.op("dve", lambda v: v.tensor_scalar(
                    seli[:, :], seli[:, :], -1, None, OP.mult), waits=later(s2))
                u2 = p.op("dve", lambda v: v.tensor_scalar(
                    seli[:, :], seli[:, :], 1, None, OP.add), waits=later(u1))
                m_th = p.op("dve", lambda v: v.tensor_scalar(
                    theta[:, :], seli[:, :], -2147483648, None, OP.mult),
                    waits=later(u2, m_th))
            else:
                u1 = p.op("dve", lambda v, b=b: v.tensor_scalar(
                    seli[:, :], seli[:, :], 1 << b, None, OP.mult), waits=later(s2))
                m_th = p.op("dve", lambda v: v.tensor_tensor(
                    theta[:, :], theta[:, :], seli[:, :], OP.add),
                    waits=later(u1, m_th))
            prev = later(m_th)

        c7 = count_ge(theta[:, 0:1], OP.is_gt, later(m_th))
        d1 = p.op("dve", lambda v: v.tensor_scalar(
            db[:, :], totb[:, :], -1.0, None, OP.mult), waits=later(c7))
        d2 = p.op("dve", lambda v: v.tensor_scalar(
            db[:, :], db[:, :], 1024.0, None, OP.add), waits=later(d1))
        def cmp_th(v, opx, dst):
            a, b = broadcast_tensor_aps(key[:, :], theta[:, 0:1])
            return v.tensor_tensor(dst, a, b, opx)
        mm1a = p.op("dve", lambda v: cmp_th(v, OP.is_gt, ge[:, :]),
                    waits=later(d2))
        mm1 = p.op("dve", lambda v: v.tensor_copy(m1f[:, :], ge[:, :]),
                   waits=later(mm1a))
        mm2a = p.op("dve", lambda v: cmp_th(v, OP.is_equal, ge[:, :]),
                    waits=later(mm1))
        mm2 = p.op("dve", lambda v: v.tensor_copy(m2f[:, :], ge[:, :]),
                   waits=later(mm2a))
        sa = p.op("dve", lambda v: v.tensor_tensor_scan(
            sc2[:, :], m2f[:, :], m2f[:, :], 0.0, OP.add, OP.bypass),
            waits=later(mm2))
        sb_ = p.op("pe", lambda t: t.matmul(
            ptb[:, 1:2], self.c_slt[:, :], sc2[:, 63:64], start=True, stop=True),
            waits=later(sa))
        sc = p.op("dve", lambda v: v.tensor_copy(offc[:, :], ptb[:, 1:2]),
                  waits=later(sb_))
        sd = p.op("dve", lambda v: v.tensor_scalar(
            sc2[:, :], sc2[:, :], offc[:, 0:1], None, OP.add), waits=later(sc))
        se = p.op("dve", lambda v: v.tensor_scalar(
            sc2[:, :], sc2[:, :], db[:, 0:1], None, OP.is_le), waits=later(sd))
        sf = p.op("dve", lambda v: v.tensor_mul(
            m2f[:, :], m2f[:, :], sc2[:, :]), waits=later(se))
        sg = p.op("dve", lambda v: v.tensor_add(
            fm[:, :], m1f[:, :], m2f[:, :]), waits=later(sf))
        pa = p.op("dve", lambda v: v.tensor_tensor_scan(
            sc1[:, :], fm[:, :], fm[:, :], 0.0, OP.add, OP.bypass),
            waits=later(sg))
        pb = p.op("pe", lambda t: t.matmul(
            ptb[:, 2:3], self.c_slt[:, :], sc1[:, 63:64], start=True, stop=True),
            waits=later(pa))
        pc = p.op("dve", lambda v: v.tensor_copy(offc[:, :], ptb[:, 2:3]),
                  waits=later(pb))
        pd = p.op("dve", lambda v: v.tensor_scalar(
            sc1[:, :], sc1[:, :], offc[:, 0:1], None, OP.add), waits=later(pc))
        pe_ = p.op("dve", lambda v: v.tensor_scalar(
            sc1[:, :], sc1[:, :], -1.0, None, OP.add), waits=later(pd))
        BIGF = float(1 << 24)
        qa = p.op("dve", lambda v: v.tensor_mul(
            tposf[:, :], sc1[:, :], fm[:, :]), waits=later(pe_))
        qb = p.op("dve", lambda v: v.tensor_scalar(
            sc2[:, :], fm[:, :], -BIGF, None, OP.mult), waits=later(qa))
        qc = p.op("dve", lambda v: v.tensor_scalar(
            sc2[:, :], sc2[:, :], BIGF, None, OP.add), waits=later(qb))
        qd = p.op("dve", lambda v: v.tensor_add(
            tposf[:, :], tposf[:, :], sc2[:, :]), waits=later(qc))
        qe = p.op("dve", lambda v: v.tensor_copy(tposi[:, :], tposf[:, :]),
                  waits=later(qd))
        ra = p.op("act", lambda s: s.activation(expv[:, :], lgt[:, :], AF.Exp),
                  waits=later(m_lg, qe))
        rb = p.op("dve", lambda v: v.tensor_mul(
            expv[:, :], expv[:, :], fm[:, :]), waits=later(ra))
        rc = p.op("dve", lambda v: v.tensor_reduce(
            cntf[:, :], expv[:, :], AX.X, OP.add), waits=later(rb))
        rd = p.op("pe", lambda t: t.matmul(
            pts[0:1, 1:2], onesc[:, :], cntf[:, :], start=True, stop=True),
            waits=later(rc))
        re = p.op("dve", lambda v: v.tensor_copy(tot[:, :], pts[0:1, 1:2]),
                  waits=later(rd))
        rf = p.op("pe", lambda t: t.matmul(
            ptb[:, 3:4], self.c_ones[0:1, :], tot[:, :], start=True, stop=True),
            waits=later(re))
        rg_ = p.op("dve", lambda v: v.tensor_copy(totb[:, :], ptb[:, 3:4]),
                   waits=later(rf))
        rh = p.op("dve", lambda v: v.reciprocal(offc[:, :], totb[:, :]),
                  waits=later(rg_))
        ri = p.op("dve", lambda v: v.tensor_scalar(
            wv[:, :], expv[:, :], offc[:, 0:1], None, OP.mult), waits=later(rh))
        pairs3 = pairs[:, :].rearrange("p (f two) -> p f two", two=2)
        za = p.op("dve", lambda v: v.tensor_copy(
            pairs3[:, :, 0:1],
            gidf_sb[:, :].rearrange("p (f one) -> p f one", one=1)),
            waits=later(m_gid, ri))
        zb = p.op("dve", lambda v: v.tensor_copy(
            pairs3[:, :, 1:2],
            wv[:, :].rearrange("p (f one) -> p f one", one=1)),
            waits=later(za))
        scm = []
        for f in range(64):
            scm.append(p.idma("tkS", d["seltab"],
                              IndirectOffsetOnAxis(tposi[:, f:f + 1], 0),
                              pairs[:, 2 * f:2 * f + 2], None,
                              bounds_check=SEL - 1, oob_is_err=False,
                              waits=later(zb, qe) if f == 0 else ()))
        m_sc = later(*scm)
        selw3 = selw[:, :].rearrange("p (s two) -> p s two", two=2)
        m_sw = p.dma("sp", "tkB", selw3,
                     d["seltab"].rearrange("(s p) two -> p s two", p=128),
                     waits=m_sc)
        va = p.op("dve", lambda v: v.tensor_copy(
            self.selint[:, :].rearrange("p (s one) -> p s one", one=1),
            selw3[:, :, 0:1]), waits=later(m_sw))
        vb = p.op("dve", lambda v: v.tensor_copy(
            self.wsel[:, :].rearrange("p (s one) -> p s one", one=1),
            selw3[:, :, 1:2]), waits=later(m_sw, va))
        ia = p.op("dve", lambda v: v.memset(itile[:, :], BIGI))
        ib = p.dma("sp", "tkB2", d["invd"].rearrange("(p f) o -> p (f o)", p=128),
                   itile[:, :], waits=later(ia))
        ivm = []
        for s in range(8):
            ivm.append(p.idma("tkS", d["invd"],
                              IndirectOffsetOnAxis(self.selint[:, s:s + 1], 0),
                              jio[:, s:s + 1], None,
                              bounds_check=T - 1, oob_is_err=False,
                              waits=later(ib, va, m_ji) if s == 0 else ()))
        m_inv = later(*ivm)
        gvm = []
        for s in range(8):
            gvm.append(p.idma("tkS", self.myinv[:, s:s + 1], None,
                              d["invd"], IndirectOffsetOnAxis(myr[:, s:s + 1], 0),
                              waits=later(m_inv, m_my) if s == 0 else ()))
        m_g0 = later(*gvm)
        wvm = [m_g0]
        for i in range(1, NC):
            wvm.append(p.op("dve", lambda v, i=i: v.tensor_scalar(
                self.myinv[:, 8 * i:8 * (i + 1)], self.myinv[:, 0:8],
                1024 * i, None, OP.add), waits=later(m_g0)))
        self.m_topk = later(vb, *wvm)
        self.dbg("lgt", lgt[:, :], [m_lg])
        self.dbg("selraw", d["seltab"], [m_sc])
        self.dbg("invraw", d["invd"], [m_inv])
        self.dbg("theta", theta[:, :], [m_th])
        self.dbg("selint", self.selint[:, :], [self.m_topk])
        self.dbg("wsel", self.wsel[:, :], [self.m_topk])
        self.dbg("myinv", self.myinv[:, :], [self.m_topk])
        self.m_ph_topk = p.fence()
        ph.close()

    def ph_block(self):
        nc, d, p = self.nc, self.d, self.p
        blockstack = contextlib.ExitStack()
        bsb = blockstack.enter_context
        pA = bsb(nc.psum_tensor("pA", [128, SEL], F32))
        pB = bsb(nc.psum_tensor("pB", [128, SEL], F32))
        pC = bsb(nc.psum_tensor("pC", [128, SEL], F32))
        pD = bsb(nc.psum_tensor("pD", [128, 2048], BF16))
        ph = contextlib.ExitStack()
        sb = ph.enter_context

        xg = [sb(nc.sbuf_tensor(f"xg{i}", [128, D], F32)) for i in range(2)]
        xnst = [sb(nc.sbuf_tensor(f"xnst{i}", [128, D], F32)) for i in range(2)]
        hpre = sb(nc.sbuf_tensor("hpre", [128, D], F32))
        xstg = sb(nc.sbuf_tensor("xstg", [128, 256], F32))
        h_fm = sb(nc.sbuf_tensor("h_fm", [128, 16 * SEL], BF16))
        st2 = sb(nc.sbuf_tensor("st2", [128, 6], F32))
        ssqb = sb(nc.sbuf_tensor("ssqb", [128, 1], F32))
        sqs = sb(nc.sbuf_tensor("sqs", [128, D], F32))
        csb = sb(nc.sbuf_tensor("csb", [128, SEL], F32))
        scb = sb(nc.sbuf_tensor("scb", [128, SEL], F32))
        wbx = sb(nc.sbuf_tensor("wbx", [128, 3 * 16 * 256], BF16))
        qf32 = sb(nc.sbuf_tensor("qf32", [128, 2 * SEL], F32))
        kf32 = sb(nc.sbuf_tensor("kf32", [128, 2 * SEL], F32))
        tmra = sb(nc.sbuf_tensor("tmra", [64, SEL], F32))
        tmrb = sb(nc.sbuf_tensor("tmrb", [64, SEL], F32))
        qbf = sb(nc.sbuf_tensor("qbf", [128, 2 * SEL], BF16))
        kbf = sb(nc.sbuf_tensor("kbf", [128, 2 * SEL], BF16))
        vbf = sb(nc.sbuf_tensor("vbf", [128, 2 * SEL], BF16))
        vtmb = sb(nc.sbuf_tensor("vtmb", [128, 2 * SEL], BF16))
        pfull = sb(nc.sbuf_tensor("pfull", [128, 8 * SEL], BF16))
        ptk = sb(nc.sbuf_tensor("ptk", [128, SEL], BF16))
        mbt = [sb(nc.sbuf_tensor(f"mbt{i}", [128, SEL], F32)) for i in range(2)]
        nmx = sb(nc.sbuf_tensor("nmx", [128, 1], F32))
        den = sb(nc.sbuf_tensor("den", [128, 8], F32))
        rden = sb(nc.sbuf_tensor("rden", [128, 1], F32))
        avh = sb(nc.sbuf_tensor("avh", [128, SEL], BF16))

        mtop = self.m_topk
        p.dma("sp", "bk0", csb[0:64, :], d["cosf"])
        p.dma("sp", "bk0", csb[64:128, :], d["sinf"])
        p.dma("sp", "bk0", scb[0:64, :], d["sinf"])
        m_cos = later(p.dma("sp", "bk0", scb[64:128, :], d["cosf"]))
        tile_fin = {}
        ev_x = {}
        ev_h = {}
        ssqb_rd = None
        for s in range(8):
            sl = s % 2
            war = [tile_fin[s - 2]] if s >= 2 else []
            m_g = p.idma(f"bgx{sl}", xg[sl][:, :], None, d["x_full"],
                         IndirectOffsetOnAxis(self.selint[:, s:s + 1], 0),
                         waits=later(mtop, *war))
            m_sq = p.op("act", lambda a, sl=sl: a.activation(
                sqs[:, :], xg[sl][:, :], AF.Square, accum_out=ssqb[:, :]),
                waits=later(m_g, ssqb_rd))
            m_sum = p.op("dve", lambda v, sl=sl: v.reduce_sum(
                st2[:, 0:1], xg[sl][:, :], AX.X), waits=later(m_g))
            m_mean = p.op("dve", lambda v: v.tensor_scalar(
                st2[:, 0:1], st2[:, 0:1], 1.0 / D, None, OP.mult),
                waits=later(m_sum))
            m_v1 = p.op("dve", lambda v: v.tensor_scalar(
                st2[:, 1:2], ssqb[:, :], 1.0 / D, None, OP.mult),
                waits=later(m_sq))
            ssqb_rd = m_v1
            m_v2 = p.op("dve", lambda v: v.tensor_mul(
                st2[:, 2:3], st2[:, 0:1], st2[:, 0:1]), waits=later(m_mean))
            m_v3 = p.op("dve", lambda v: v.tensor_sub(
                st2[:, 1:2], st2[:, 1:2], st2[:, 2:3]), waits=later(m_v1, m_v2))
            m_v4 = p.op("dve", lambda v: v.tensor_scalar(
                st2[:, 1:2], st2[:, 1:2], 1e-5, None, OP.add), waits=later(m_v3))
            m_rc = p.op("dve", lambda v: v.reciprocal(st2[:, 2:3], st2[:, 1:2]),
                        waits=later(m_v4))
            m_rs = p.op("act", lambda a: a.activation(
                st2[:, 3:4], st2[:, 2:3], AF.Sqrt), waits=later(m_rc))
            m_xn = p.op("dve", lambda v, sl=sl: v.tensor_scalar(
                xnst[sl][:, :], xg[sl][:, :], st2[:, 0:1], st2[:, 3:4],
                OP.subtract, OP.mult), waits=later(m_rs))
            m_s2 = p.op("act", lambda a, sl=sl: a.activation(
                sqs[:, :], xnst[sl][:, :], AF.Square, accum_out=ssqb[:, :]),
                waits=later(m_xn))
            m_u1 = p.op("dve", lambda v: v.tensor_scalar(
                st2[:, 4:5], ssqb[:, :], 1.0 / D, None, OP.mult),
                waits=later(m_s2))
            ssqb_rd = m_u1
            m_u2 = p.op("dve", lambda v: v.tensor_scalar(
                st2[:, 4:5], st2[:, 4:5], 1e-6, None, OP.add), waits=later(m_u1))
            m_u3 = p.op("dve", lambda v: v.reciprocal(st2[:, 5:6], st2[:, 4:5]),
                        waits=later(m_u2))
            m_u4 = p.op("act", lambda a: a.activation(
                st2[:, 4:5], st2[:, 5:6], AF.Sqrt), waits=later(m_u3))
            m_hp = p.op("dve", lambda v, sl=sl: v.tensor_scalar(
                hpre[:, :], xnst[sl][:, :], st2[:, 4:5], None, OP.mult),
                waits=later(m_u4))
            last_x = None
            last_h = None
            for i in range(16):
                ps = i % 2
                warx = [ev_x[(s, i - 2)]] if i >= 2 else (
                    [ev_x[(s - 1, 14 + i)]] if s >= 1 else [])
                m_t1 = p.op("pe", lambda t, sl=sl, i=i, ps=ps: t.transpose(
                    pA[:, 512 * ps:512 * ps + 128],
                    xnst[sl][:, 128 * i:128 * (i + 1)], self.c_idnf[:, :]),
                    waits=later(m_xn, *warx))
                m_e1 = p.op("dve", lambda v, ps=ps: v.tensor_copy(
                    xstg[:, 128 * ps:128 * (ps + 1)], pA[:, 512 * ps:512 * ps + 128]),
                    waits=later(m_t1))
                m_w1 = p.dma("sp", f"xns{ps}",
                             d["xnself_d"][128 * i:128 * (i + 1),
                                           128 * s:128 * (s + 1)],
                             xstg[:, 128 * ps:128 * (ps + 1)], waits=later(m_e1))
                ev_x[(s, i)] = later(m_e1, m_w1)
                last_x = ev_x[(s, i)]
            for i in range(16):
                ps = i % 2
                warh = [ev_h[(s, i - 2)]] if i >= 2 else (
                    [ev_h[(s - 1, 14 + i)]] if s >= 1 else [])
                m_t2 = p.op("pe", lambda t, i=i, ps=ps: t.transpose(
                    pA[:, 512 * ps:512 * ps + 128],
                    hpre[:, 128 * i:128 * (i + 1)], self.c_idnf[:, :]),
                    waits=later(m_hp, last_x, *warh))
                m_e2 = p.op("dve", lambda v, i=i, s=s, ps=ps: v.tensor_scalar(
                    h_fm[:, SEL * i + 128 * s:SEL * i + 128 * (s + 1)],
                    pA[:, 512 * ps:512 * ps + 128],
                    self.c_g1[:, i:i + 1], None, OP.mult), waits=later(m_t2))
                ev_h[(s, i)] = m_e2
                last_h = m_e2
            tile_fin[s] = later(last_x, last_h, m_s2, m_hp)
        m_hfm = later(*[ev_h[(s, 15)] for s in range(8)])
        m_xns_all = later(*[ev_x[(s, i)] for s in range(8) for i in (14, 15)])

        m_wb = []
        for wi, wname in enumerate(("wq_b", "wk_b", "wv_b")):
            m_wb.append(p.dma("pool", "bwq",
                              wbx[:, 4096 * wi:4096 * (wi + 1)]
                              .rearrange("p (dc o) -> p dc o", o=256),
                              d[wname].rearrange("(dc p) o -> p dc o", p=128)))
        m_wb = later(*m_wb)
        ev_qkv = {}
        prev_blk = None
        for wi in range(3):
            for oc in range(2):
                mm = None
                w0 = later(m_wb, m_hfm, prev_blk)
                for dc in range(16):
                    for half in range(2):
                        mm = p.op("pe", lambda t, wi=wi, oc=oc, dc=dc, half=half:
                            t.matmul(
                                pA[:, 512 * half:512 * (half + 1)],
                                wbx[:, 4096 * wi + 256 * dc + 128 * oc:
                                    4096 * wi + 256 * dc + 128 * (oc + 1)],
                                h_fm[:, SEL * dc + 512 * half:SEL * dc + 512 * (half + 1)],
                                start=(dc == 0), stop=(dc == 15)), waits=w0)
                        w0 = ()
                if wi < 2:
                    dstf = qf32 if wi == 0 else kf32
                    m_ev = p.op("dve", lambda v, dstf=dstf, oc=oc: v.tensor_copy(
                        dstf[:, SEL * oc:SEL * (oc + 1)], pA[:, :]), waits=later(mm))
                else:
                    m_ev = p.op("dve", lambda v, oc=oc: v.tensor_copy(
                        vbf[:, SEL * oc:SEL * (oc + 1)], pA[:, :]), waits=later(mm))
                ev_qkv[(wi, oc)] = m_ev
                prev_blk = m_ev
        rope_done = []
        for (srcf, dstb, wi) in ((qf32, qbf, 0), (kf32, kbf, 1)):
            for oc in range(2):
                base = SEL * oc
                mk = ev_qkv[(wi, oc)]
                r1 = p.op("dve", lambda v, srcf=srcf, base=base: v.tensor_mul(
                    tmra[:, :], srcf[0:64, base:base + SEL], csb[0:64, :]),
                    waits=later(mk, m_cos))
                r2 = p.op("dve", lambda v, srcf=srcf, base=base: v.tensor_mul(
                    tmrb[:, :], srcf[64:128, base:base + SEL], csb[64:128, :]),
                    waits=later(r1))
                r3 = p.op("dve", lambda v, dstb=dstb, base=base: v.tensor_sub(
                    dstb[0:64, base:base + SEL], tmra[:, :], tmrb[:, :]),
                    waits=later(r2))
                r4 = p.op("dve", lambda v, srcf=srcf, base=base: v.tensor_mul(
                    tmra[:, :], srcf[0:64, base:base + SEL], scb[0:64, :]),
                    waits=later(r3))
                r5 = p.op("dve", lambda v, srcf=srcf, base=base: v.tensor_mul(
                    tmrb[:, :], srcf[64:128, base:base + SEL], scb[64:128, :]),
                    waits=later(r4))
                r6 = p.op("dve", lambda v, dstb=dstb, base=base: v.tensor_add(
                    dstb[64:128, base:base + SEL], tmra[:, :], tmrb[:, :]),
                    waits=later(r5))
                rope_done.append(r6)
        m_rope = later(*rope_done)
        vt_done = []
        prev_vt = {}
        for oc in range(2):
            for kc in range(8):
                ps = kc % 2
                idx = oc * 8 + kc
                war = [prev_vt[idx - 2]] if idx >= 2 else []
                m_t = p.op("pe", lambda t, oc=oc, kc=kc, ps=ps: t.transpose(
                    pD[:, 1024 * ps:1024 * ps + 128],
                    vbf[:, SEL * oc + 128 * kc:SEL * oc + 128 * (kc + 1)],
                    self.c_idnb[:, :]),
                    waits=later(ev_qkv[(2, oc)], *war))
                m_c = p.op("dve", lambda v, oc=oc, kc=kc, ps=ps: v.tensor_copy(
                    vtmb[:, SEL * oc + 128 * kc:SEL * oc + 128 * (kc + 1)],
                    pD[:, 1024 * ps:1024 * ps + 128]), waits=later(m_t))
                prev_vt[idx] = m_c
                vt_done.append(m_c)
        m_vtm = later(*vt_done)

        scl = 1.0 / float(np.sqrt(128.0))
        av_st = []
        mb_free = {}
        exp_marks = {}
        rd_marks = {}
        for h in range(2):
            pf_norm = {}
            for qs in range(8):
                sl = qs % 2
                mwar = [mb_free[(h, qs - 2)]] if (h, qs - 2) in mb_free else (
                    [mb_free[(h - 1, 6 + qs)]] if h >= 1 else [])
                m_mb = p.dma("sp", f"mb{sl}", mbt[sl][:, :],
                             d["maskb"][128 * qs:128 * (qs + 1), :],
                             waits=later(*mwar))
                sc_w = later(m_rope, *( [exp_marks[(h, qs - 1)]] if qs >= 1 else
                                        ([exp_marks[(h - 1, 7)]] if h >= 1 else []) ))
                ms = None
                for half in range(2):
                    ms = p.op("pe", lambda t, h=h, qs=qs, half=half: t.matmul(
                        pB[:, 512 * half:512 * (half + 1)],
                        qbf[:, SEL * h + 128 * qs:SEL * h + 128 * (qs + 1)],
                        kbf[:, SEL * h + 512 * half:SEL * h + 512 * (half + 1)],
                        start=True, stop=True), waits=sc_w)
                    sc_w = ()
                m_msk = p.op("dve", lambda v, sl=sl, scl=scl: v.scalar_tensor_tensor(
                    pB[:, :], pB[:, :], scl, mbt[sl][:, :], OP.mult, OP.add),
                    waits=later(ms, m_mb))
                m_mx = p.op("dve", lambda v: v.tensor_reduce(
                    nmx[:, :], pB[:, :], AX.X, OP.max, negate=True),
                    waits=later(m_msk))
                m_ex = p.op("act", lambda a, h=h, qs=qs: a.activation(
                    pfull[:, SEL * qs:SEL * (qs + 1)], pB[:, :], AF.Exp,
                    bias=nmx[:, 0:1], accum_out=den[:, qs:qs + 1]),
                    waits=later(m_mx, rd_marks.get((h - 1, qs))))
                exp_marks[(h, qs)] = m_ex
                m_rd = p.op("dve", lambda v, qs=qs: v.reciprocal(
                    rden[:, :], den[:, qs:qs + 1]), waits=later(m_ex))
                rd_marks[(h, qs)] = m_rd
                m_nm = p.op("dve", lambda v, qs=qs: v.tensor_scalar(
                    pfull[:, SEL * qs:SEL * (qs + 1)],
                    pfull[:, SEL * qs:SEL * (qs + 1)],
                    rden[:, 0:1], None, OP.mult), waits=later(m_rd))
                pf_norm[qs] = m_nm
                mb_free[(h, qs)] = later(m_msk)
            m_av = None
            ptk_used = None
            for kc in range(8):
                tps = []
                for qs in range(8):
                    ps = qs % 2
                    m_t = p.op("pe", lambda t, qs=qs, kc=kc, ps=ps: t.transpose(
                        pD[:, 1024 * ps:1024 * ps + 128],
                        pfull[:, SEL * qs + 128 * kc:SEL * qs + 128 * (kc + 1)],
                        self.c_idnb[:, :]),
                        waits=later(pf_norm[qs],
                                    *( [tps[-2][1]] if len(tps) >= 2 else
                                       ([ptk_used] if qs <= 1 and ptk_used else []) )))
                    m_c = p.op("dve", lambda v, qs=qs, ps=ps: v.tensor_copy(
                        ptk[:, 128 * qs:128 * (qs + 1)],
                        pD[:, 1024 * ps:1024 * ps + 128]),
                        waits=later(m_t, ptk_used if qs == 0 else None))
                    tps.append((m_t, m_c))
                avw = later(tps[-1][1], tps[-2][1])
                for half in range(2):
                    m_av = p.op("pe", lambda t, h=h, kc=kc, half=half: t.matmul(
                        pC[:, 512 * half:512 * (half + 1)],
                        vtmb[:, SEL * h + 128 * kc:SEL * h + 128 * (kc + 1)],
                        ptk[:, 512 * half:512 * (half + 1)],
                        start=(kc == 0), stop=(kc == 7)), waits=avw)
                    avw = ()
                ptk_used = m_av
            m_eav = p.op("dve", lambda v: v.tensor_copy(avh[:, :], pC[:, :]),
                         waits=later(m_av, av_st[-1] if av_st else None))
            av_st.append(p.dma("sp", "avS", d["avags"][128 * h:128 * (h + 1), :],
                               avh[:, :], waits=later(m_eav)))
        m_ag_av = p.coll("AllGather", OP.bypass, [d["avags"]],
                         [d["avag"]], waits=later(*av_st))

        bf1 = p.fence()
        ph.close()
        ph = contextlib.ExitStack()
        sb = ph.enter_context
        avf = [sb(nc.sbuf_tensor(f"avf{i}", [128, SEL], BF16)) for i in range(2)]
        wob_b = sb(nc.sbuf_tensor("wob_b", [128, 16 * 256], BF16))
        aocols = sb(nc.sbuf_tensor("aocols", [128, D], F32))
        xnmy = sb(nc.sbuf_tensor("xnmy", [128, D], F32))
        xblk = sb(nc.sbuf_tensor("xblk", [128, D], F32))
        myd = sb(nc.sbuf_tensor("myd", [128, 2], I32))
        ssqmy = sb(nc.sbuf_tensor("ssqmy", [1, SEL], F32))
        rr2 = sb(nc.sbuf_tensor("rr2", [1, SEL], F32))
        g2sb = sb(nc.sbuf_tensor("g2sb", [128, 2], F32))
        h2c = sb(nc.sbuf_tensor("h2c", [128, D], BF16))
        h2g = sb(nc.sbuf_tensor("h2g", [128, 16 * SEL], BF16))
        w1s = [sb(nc.sbuf_tensor(f"w1s{i}", [128, 2048], BF16)) for i in range(2)]
        w3s = [sb(nc.sbuf_tensor(f"w3s{i}", [128, 2048], BF16)) for i in range(2)]
        u_bf = sb(nc.sbuf_tensor("u_bf", [128, SEL], BF16))
        v_bf = sb(nc.sbuf_tensor("v_bf", [128, SEL], BF16))
        gsb = sb(nc.sbuf_tensor("gsb", [128, SEL], BF16))
        w2s = [sb(nc.sbuf_tensor(f"w2s{i}", [128, 8 * 128], BF16)) for i in range(2)]
        ggt = [sb(nc.sbuf_tensor(f"ggt{i}", [128, SEL], BF16)) for i in range(2)]
        ycols = sb(nc.sbuf_tensor("ycols", [128, D], F32))
        xout = sb(nc.sbuf_tensor("xout", [128, D], F32))
        xwst = [sb(nc.sbuf_tensor(f"xwst{i}", [128, 128], F32)) for i in range(2)]
        sqs2 = sb(nc.sbuf_tensor("sqs2", [128, D], F32))
        onesc_blk = sb(nc.sbuf_tensor("onesc_blk", [128, 1], F32))
        for e in ENGINES:
            p.wait(e, bf1)
        m_ob = p.op("dve", lambda v: v.memset(onesc_blk[:, :], 1.0))
        m_myd = later(p.dma("sp", "bk1", myd[:, :], d["mydims"]),
                      p.dma("sp", "bk1", g2sb[:, :], d["g2c"]))

        m_wob = p.dma("pool", "bwo",
                      wob_b[:, :].rearrange("p (dc o) -> p dc o", o=256),
                      d["wo_b"].rearrange("(dc p) o -> p dc o", p=128))
        avf_ld = {}
        m_ao = {}
        prev_ao = None
        for oc in range(2):
            mm = None
            w0 = later(m_wob, m_ag_av, prev_ao)
            for dc in range(16):
                sl = dc % 2
                war = [avf_ld[(oc, dc - 2)][1]] if (oc, dc - 2) in avf_ld else (
                    [avf_ld[(oc - 1, 14 + dc)][1]] if oc >= 1 else [])
                m_l = p.dma("sp", f"avf{sl}", avf[sl][:, :],
                            d["avag"][128 * dc:128 * (dc + 1), :],
                            waits=later(m_ag_av, *war))
                for half in range(2):
                    mm = p.op("pe", lambda t, oc=oc, dc=dc, half=half, sl=sl:
                        t.matmul(
                            pA[:, 512 * half:512 * (half + 1)],
                            wob_b[:, 256 * dc + 128 * oc:256 * dc + 128 * (oc + 1)],
                            avf[sl][:, 512 * half:512 * (half + 1)],
                            start=(dc == 0), stop=(dc == 15)),
                        waits=later(m_l, w0))
                    w0 = ()
                avf_ld[(oc, dc)] = (m_l, mm)
            m_ao[oc] = p.op("dve", lambda v, oc=oc: v.tensor_copy(
                aocols[:, SEL * oc:SEL * (oc + 1)], pA[:, :]), waits=later(mm))
            prev_ao = m_ao[oc]
        gx = []
        for oc in range(2):
            gx.append(p.idma("bxn", xnmy[:, SEL * oc:SEL * (oc + 1)], None,
                             d["xnself_d"],
                             IndirectOffsetOnAxis(myd[:, oc:oc + 1], 0),
                             waits=later(m_xns_all, m_myd) if oc == 0 else ()))
        m_xb = p.op("dve", lambda v: v.tensor_add(
            xblk[:, :], xnmy[:, :], aocols[:, :]),
            waits=later(m_ao[0], m_ao[1], *gx))

        m_q2 = p.op("act", lambda a: a.activation(
            sqs2[:, :], xblk[:, :], AF.Square), waits=later(m_xb))
        mm = None
        w0 = later(m_q2, m_ob)
        for oc in range(2):
            for half in range(2):
                mm = p.op("pe", lambda t, oc=oc, half=half: t.matmul(
                    pA[0:1, 512 * half:512 * (half + 1)], onesc_blk[:, :],
                    sqs2[:, SEL * oc + 512 * half:SEL * oc + 512 * (half + 1)],
                    start=(oc == 0), stop=(oc == 1)), waits=w0)
                w0 = ()
        m_sq2 = p.op("dve", lambda v: v.tensor_copy(ssqmy[:, :], pA[0:1, :]),
                     waits=later(mm))
        m_sst = p.dma("sp", "bss", d["ssqag"], ssqmy[:, :], waits=later(m_sq2))
        m_ar = p.coll("AllReduce", OP.add, [d["ssqag"]], [d["ssqag"]],
                      waits=later(m_sst))
        m_sld = p.dma("sp", "bss", rr2[:, :], d["ssqag"], waits=later(m_ar))
        m_n1 = p.op("dve", lambda v: v.tensor_scalar(
            rr2[:, :], rr2[:, :], 1.0 / D, None, OP.mult), waits=later(m_sld))
        m_n2 = p.op("dve", lambda v: v.tensor_scalar(
            rr2[:, :], rr2[:, :], 1e-6, None, OP.add), waits=later(m_n1))
        m_n3 = p.op("dve", lambda v: v.reciprocal(rr2[:, :], rr2[:, :]),
                    waits=later(m_n2))
        m_n4 = p.op("act", lambda a: a.activation(rr2[:, :], rr2[:, :], AF.Sqrt),
                    waits=later(m_n3))
        bc = None
        w0 = later(m_n4)
        for half in range(2):
            bc = p.op("pe", lambda t, half=half: t.matmul(
                pB[:, 512 * half:512 * (half + 1)], self.c_ones[0:1, :],
                rr2[0:1, 512 * half:512 * (half + 1)], start=True, stop=True),
                waits=w0)
            w0 = ()
        h2m = []
        for oc in range(2):
            m_h1 = p.op("dve", lambda v, oc=oc: v.tensor_mul(
                sqs2[:, SEL * oc:SEL * (oc + 1)], xblk[:, SEL * oc:SEL * (oc + 1)],
                pB[:, :]), waits=later(bc))
            m_h2 = p.op("dve", lambda v, oc=oc: v.tensor_scalar(
                h2c[:, SEL * oc:SEL * (oc + 1)], sqs2[:, SEL * oc:SEL * (oc + 1)],
                g2sb[:, oc:oc + 1], None, OP.mult), waits=later(m_h1))
            h2m.append(p.dma("sp", "bh2", d["h2ags"][128 * oc:128 * (oc + 1), :],
                             h2c[:, SEL * oc:SEL * (oc + 1)], waits=later(m_h2)))
        m_ag_h2 = p.coll("AllGather", OP.bypass, [d["h2ags"]],
                         [d["h2ag"]], waits=later(*h2m))

        h2l = []
        for dc in range(16):
            h2l.append(p.dma("sp", "bh2l", h2g[:, SEL * dc:SEL * (dc + 1)],
                             d["h2ag"][128 * dc:128 * (dc + 1), :],
                             waits=later(m_ag_h2) if dc == 0 else ()))
        m_h2g = later(*h2l)
        g_st = []
        w1_ld = {}
        prev_g = None
        for ho in range(8):
            sl = ho % 2
            war = [w1_ld[ho - 2]] if ho >= 2 else []
            m_lw1 = p.dma("pool", f"bw1{sl}",
                          w1s[sl][:, :].rearrange("p (dc o) -> p dc o", o=128),
                          d["w1_b"][:, 128 * ho:128 * (ho + 1)]
                          .rearrange("(dc p) o -> p dc o", p=128),
                          waits=later(*war))
            m_lw3 = p.dma("pool", f"bw3{sl}",
                          w3s[sl][:, :].rearrange("p (dc o) -> p dc o", o=128),
                          d["w3_b"][:, 128 * ho:128 * (ho + 1)]
                          .rearrange("(dc p) o -> p dc o", p=128))
            mm = None
            w0 = later(m_lw1, m_h2g, prev_g)
            for dc in range(16):
                for half in range(2):
                    mm = p.op("pe", lambda t, sl=sl, dc=dc, half=half: t.matmul(
                        pA[:, 512 * half:512 * (half + 1)],
                        w1s[sl][:, 128 * dc:128 * (dc + 1)],
                        h2g[:, SEL * dc + 512 * half:SEL * dc + 512 * (half + 1)],
                        start=(dc == 0), stop=(dc == 15)), waits=w0)
                    w0 = ()
            m_si = p.op("act", lambda a: a.activation(
                u_bf[:, :], pA[:, :], AF.Silu), waits=later(mm))
            mm3 = None
            w0 = later(m_lw3, m_si)
            for dc in range(16):
                for half in range(2):
                    mm3 = p.op("pe", lambda t, sl=sl, dc=dc, half=half: t.matmul(
                        pA[:, 512 * half:512 * (half + 1)],
                        w3s[sl][:, 128 * dc:128 * (dc + 1)],
                        h2g[:, SEL * dc + 512 * half:SEL * dc + 512 * (half + 1)],
                        start=(dc == 0), stop=(dc == 15)), waits=w0)
                    w0 = ()
            w1_ld[ho] = mm3
            m_vb = p.op("dve", lambda v: v.tensor_copy(v_bf[:, :], pA[:, :]),
                        waits=later(mm3))
            m_g = p.op("dve", lambda v: v.tensor_mul(
                gsb[:, :], u_bf[:, :], v_bf[:, :]),
                waits=later(m_vb, g_st[-1] if g_st else None))
            prev_g = m_g
            g_st.append(p.dma("sp", "bgS", d["gags"][128 * ho:128 * (ho + 1), :],
                              gsb[:, :], waits=later(m_g)))
        m_ag_g = p.coll("AllGather", OP.bypass, [d["gags"]],
                        [d["gag"]], waits=later(*g_st))

        prev_y = None
        w2_ld = {}
        gg_ld = {}
        for oc in range(2):
            mm = None
            w0 = later(m_ag_g, prev_y)
            for hp in range(8):
                sl = hp % 2
                war = [w2_ld[(oc, hp - 2)]] if (oc, hp - 2) in w2_ld else (
                    [w2_ld[(oc - 1, 6 + hp)]] if oc >= 1 else [])
                m_w2 = p.dma("pool", f"bw2{sl}",
                             w2s[sl][:, :].rearrange("p (dc o) -> p dc o", o=128),
                             d["w2_b"][1024 * hp:1024 * (hp + 1),
                                       128 * oc:128 * (oc + 1)]
                             .rearrange("(dc p) o -> p dc o", p=128),
                             waits=later(*war))
                last_mm_h = None
                for dci in range(8):
                    dc = hp * 8 + dci
                    gsl = dc % 2
                    gwar = [gg_ld[(oc, dc - 2)][1]] if (oc, dc - 2) in gg_ld else (
                        [gg_ld[(oc - 1, 62 + dc)][1]] if oc >= 1 else [])
                    m_gl = p.dma("sp", f"gg{gsl}", ggt[gsl][:, :],
                                 d["gag"][128 * dc:128 * (dc + 1), :],
                                 waits=later(m_ag_g, *gwar))
                    for half in range(2):
                        mm = p.op("pe", lambda t, sl=sl, dci=dci, half=half,
                                  gsl=gsl, dc=dc: t.matmul(
                            pA[:, 512 * half:512 * (half + 1)],
                            w2s[sl][:, 128 * dci:128 * (dci + 1)],
                            ggt[gsl][:, 512 * half:512 * (half + 1)],
                            start=(dc == 0), stop=(dc == 63)),
                            waits=later(m_w2, m_gl, w0))
                        w0 = ()
                    gg_ld[(oc, dc)] = (m_gl, mm)
                    last_mm_h = mm
                w2_ld[(oc, hp)] = last_mm_h
            m_ye = p.op("dve", lambda v, oc=oc: v.tensor_copy(
                ycols[:, SEL * oc:SEL * (oc + 1)], pA[:, :]), waits=later(mm))
            prev_y = m_ye
        m_xo = p.op("dve", lambda v: v.tensor_add(
            xout[:, :], xblk[:, :], ycols[:, :]), waits=later(prev_y, m_xb))

        xw_st = []
        prev_tw = {}
        for oc in range(2):
            for s in range(8):
                ps = (oc * 8 + s) % 2
                idx = oc * 8 + s
                war = [prev_tw[idx - 2]] if idx >= 2 else []
                m_t = p.op("pe", lambda t, oc=oc, s=s, ps=ps: t.transpose(
                    pA[:, 512 * ps:512 * ps + 128],
                    xout[:, SEL * oc + 128 * s:SEL * oc + 128 * (s + 1)],
                    self.c_idnf[:, :]), waits=later(m_xo, *war))
                m_c = p.op("dve", lambda v, s=s, ps=ps: v.tensor_scalar(
                    xwst[ps][:, :], pA[:, 512 * ps:512 * ps + 128],
                    self.wsel[:, s:s + 1], None, OP.mult),
                    waits=later(m_t, *( [prev_tw[idx - 2]] if idx >= 2 else [] )))
                m_s = p.dma("sp", f"xw{ps}",
                            d["xwags"][128 * s:128 * (s + 1),
                                       128 * oc:128 * (oc + 1)],
                            xwst[ps][:, :], waits=later(m_c))
                prev_tw[idx] = later(m_c, m_s)
                xw_st.append(m_s)
        self.m_ag_xw = p.coll("AllGather", OP.bypass, [d["xwags"]],
                              [d["xwag"]], waits=later(*xw_st))
        self.dbg("xnmy", xnmy[:, :], [m_xb])
        self.dbg("aocols", aocols[:, :], [m_xb])
        self.dbg("xblk", xblk[:, :], [m_xb])
        self.dbg("xout", xout[:, :], [m_xo])
        self.m_ph_blk = p.fence()
        ph.close()

    def ph_epilogue(self):
        nc, d, p = self.nc, self.d, self.p
        ph = contextlib.ExitStack()
        sb = ph.enter_context
        ex = [sb(nc.sbuf_tensor(f"ex{i}", [128, D], F32)) for i in range(2)]
        ea = [sb(nc.sbuf_tensor(f"ea{i}", [128, D], F32)) for i in range(2)]
        ec = [sb(nc.sbuf_tensor(f"ec{i}", [128, D], F32)) for i in range(2)]
        exn = sb(nc.sbuf_tensor("exn", [128, D], F32))
        eo = [sb(nc.sbuf_tensor(f"eo{i}", [128, D], F32)) for i in range(2)]
        fin = {}
        prev_m2 = None
        for mp in range(NM):
            sl = mp % 2
            war = [fin[mp - 2]] if mp >= 2 else []
            m_x = p.dma("sp", f"ex{sl}", ex[sl][:, :],
                        d["x_shard"][128 * mp:128 * (mp + 1), :],
                        waits=later(*war))
            m_a = p.dma("sp", f"ea{sl}", ea[sl][:, :],
                        d["attno_d"][128 * mp:128 * (mp + 1), :],
                        waits=later(self.m_attno) if mp == 0 else ())
            m_z = p.op("dve", lambda v, sl=sl: v.memset(ec[sl][:, :], 0.0),
                       waits=later(*war))
            gm = []
            for i in range(NC):
                gm.append(p.idma(f"ctr{sl}", ec[sl][:, 256 * i:256 * (i + 1)],
                                 None, d["xwag"],
                                 IndirectOffsetOnAxis(
                                     self.myinv[:, 8 * i + mp:8 * i + mp + 1], 0),
                                 bounds_check=NC * SEL - 1, oob_is_err=False,
                                 compute_op=OP.add,
                                 waits=later(m_z, self.m_ag_xw) if i == 0 else ()))
            m_xn = p.op("dve", lambda v, sl=sl, mp=mp: v.tensor_scalar(
                exn[:, :], ex[sl][:, :], self.c_stats[:, 2 * mp:2 * mp + 1],
                self.c_stats[:, 2 * mp + 1:2 * mp + 2], OP.subtract, OP.mult),
                waits=later(m_x, prev_m2, *war))
            m_1 = p.op("dve", lambda v, sl=sl: v.tensor_add(
                eo[sl][:, :], ex[sl][:, :], ea[sl][:, :]), waits=later(m_x, m_a, m_xn))
            m_2 = p.op("dve", lambda v, sl=sl: v.tensor_add(
                eo[sl][:, :], eo[sl][:, :], exn[:, :]), waits=later(m_1))
            prev_m2 = m_2
            m_3 = p.op("dve", lambda v, sl=sl: v.tensor_add(
                eo[sl][:, :], eo[sl][:, :], ec[sl][:, :]),
                waits=later(m_2, *gm))
            m_st = p.dma("sp", f"yS{sl}", self.out_y[128 * mp:128 * (mp + 1), :],
                         eo[sl][:, :], waits=later(m_3))
            fin[mp] = later(m_st, m_3)
        ph.close()


def _prep_inputs(c, inp):
    f32 = np.float32
    x = np.ascontiguousarray(np.asarray(inp["L0_attn_x"], dtype=f32)[0])
    xr = x.reshape(64, 128, D)
    x_shard = np.ascontiguousarray(xr[c::NC].reshape(TL, D))
    eye = np.eye(128, dtype=f32)
    slt = np.triu(np.ones((128, 128), dtype=f32), 1)
    jj = np.arange(KB)[None, :, None]
    pp = np.arange(128)[:, None, None]
    rr = np.arange(128)[None, None, :]
    diag = (128 * jj + pp <= 128 * c + rr).astype(f32).reshape(128, 1024)
    myrows = (128 * (8 * np.arange(NM)[None, :] + c)
              + np.arange(128)[:, None]).astype(np.int32)
    mydims = (256 * c + 128 * np.arange(2)[None, :]
              + np.arange(128)[:, None]).astype(np.int32)
    jiota = (np.arange(128)[:, None] + 128 * np.arange(8)[None, :]).astype(np.int32)
    gidf = (np.arange(128)[:, None] * 64 + np.arange(64)[None, :]).astype(f32)

    def fT(F, nh, base):
        rows = F.shape[0]
        nch = rows // 128
        out = np.zeros((8, nch, nh, 128), f32)
        for h in range(nh):
            Fh = F[:, 8 * h:8 * h + 8].T
            out[:, :, h, :] = Fh.reshape(8, nch, 128)
        return np.ascontiguousarray(out.reshape(8, nch * nh * 128))

    def pR(P):
        nh = P.shape[0]
        out = np.zeros((8, nh * 64), f32)
        for h in range(nh):
            out[:, 64 * h:64 * (h + 1)] = P[h]
        return np.ascontiguousarray(out)

    freqs = np.asarray(inp["L0_ffn_freqs_cis"], f32)
    g1 = np.ascontiguousarray(np.asarray(inp["L0_ffn_g1"], f32).reshape(16, 128).T)
    g2 = np.ascontiguousarray(np.asarray(inp["L0_ffn_g2"], f32)[256 * c:256 * (c + 1)]
                              .reshape(2, 128).T)
    return {
        "x_shard": x_shard,
        "x_full": x,
        "wq_m": np.asarray(inp["L0_attn_W_Q"], f32),
        "wk_m": np.asarray(inp["L0_attn_W_K"], f32),
        "wv_m": np.asarray(inp["L0_attn_W_V"], f32),
        "wo_m": np.asarray(inp["L0_attn_W_O"], f32),
        "fk0t": fT(np.asarray(inp["L0_attn_FK0"], f32), 4, 256),
        "fv0t": fT(np.asarray(inp["L0_attn_FV0"], f32), 4, 256),
        "fk1t": fT(np.asarray(inp["L0_attn_FK1"], f32), 8, 1024),
        "fv1t": fT(np.asarray(inp["L0_attn_FV1"], f32), 8, 1024),
        "pk0": pR(np.asarray(inp["L0_attn_PK0"], f32)),
        "pv0": pR(np.asarray(inp["L0_attn_PV0"], f32)),
        "pk1": pR(np.asarray(inp["L0_attn_PK1"], f32)),
        "pv1": pR(np.asarray(inp["L0_attn_PV1"], f32)),
        "wrt": np.ascontiguousarray(np.asarray(inp["L0_ffn_w_router"], f32).reshape(16, 128).T),
        "wq_b": np.ascontiguousarray(np.asarray(inp["L0_ffn_wq"], f32)[:, 256 * c:256 * (c + 1)]),
        "wk_b": np.ascontiguousarray(np.asarray(inp["L0_ffn_wk"], f32)[:, 256 * c:256 * (c + 1)]),
        "wv_b": np.ascontiguousarray(np.asarray(inp["L0_ffn_wv"], f32)[:, 256 * c:256 * (c + 1)]),
        "wo_b": np.ascontiguousarray(np.asarray(inp["L0_ffn_wo"], f32)[:, 256 * c:256 * (c + 1)]),
        "w1_b": np.ascontiguousarray(np.asarray(inp["L0_ffn_w1"], f32)[:, 1024 * c:1024 * (c + 1)]),
        "w3_b": np.ascontiguousarray(np.asarray(inp["L0_ffn_w3"], f32)[:, 1024 * c:1024 * (c + 1)]),
        "w2_b": np.ascontiguousarray(np.asarray(inp["L0_ffn_w2"], f32)[:, 256 * c:256 * (c + 1)]),
        "g1c": g1, "g2c": np.ascontiguousarray(g2),
        "cosf": np.ascontiguousarray(freqs[:, :, 0].T),
        "sinf": np.ascontiguousarray(freqs[:, :, 1].T),
        "maskb": np.asarray(inp["L0_ffn_mask"], f32),
        "diagm": diag.astype(ml_dtypes.bfloat16),
        "idnf": eye, "idnb": eye.astype(ml_dtypes.bfloat16),
        "sltm": slt,
        "onesr": np.ones((1, 128), f32),
        "gidf": gidf,
        "myrows": myrows, "mydims": mydims, "jiota": jiota,
    }


_NC_CACHE = {}


def kernel(**inputs):
    if "full" not in _NC_CACHE:
        _NC_CACHE["full"] = Builder(None).build()
    nc = _NC_CACHE["full"]
    in_maps = [_prep_inputs(c, inputs) for c in range(NC)]
    res = run_bass_kernel_spmd(nc, in_maps, core_ids=list(range(NC)))
    out = np.zeros((64, 128, D), np.float32)
    for c in range(NC):
        out[c::NC] = res.results[c]["y"].reshape(NM, 128, D)
    return out.reshape(1, T, D)
